# version 11
# baseline (speedup 1.0000x reference)
import numpy as np
import concourse.bass as bass
import concourse.mybir as mybir
from concourse.bass_utils import run_bass_kernel_spmd
from concourse.dve_spec import Spec, Src0, Src1, C0, C1, lower, maxx, minn
from concourse.dve_op import DveOp
import concourse.dve_ops as dvo

F32 = mybir.dt.float32
BF16 = mybir.dt.bfloat16
FP8 = mybir.dt.float8e4
I32 = mybir.dt.int32
AF = mybir.ActivationFunctionType
ALU = mybir.AluOpType
AX = mybir.AxisListType

SEQ, ST, LC, D8 = 4, 4, 4, 8
NG = SEQ * ST
NT = NG * LC


def _e4m3fn_quant(v):
    dt = np.asarray(v).dtype
    v64 = np.clip(np.asarray(v, dtype=np.float64), -448.0, 448.0)
    a = np.abs(v64)
    with np.errstate(divide="ignore"):
        e = np.floor(np.log2(np.maximum(a, 1e-300)))
    e = np.clip(e, -6.0, 8.0)
    step = np.exp2(e - 3.0)
    q = np.round(v64 / step) * step
    q = np.where(a == 0.0, 0.0, q)
    return np.clip(q, -448.0, 448.0).astype(dt)


def _qe4_ref(in0, in1, s0, s1, imm2):
    v = in0 if in1 is None else in0 - in1
    return _e4m3fn_quant(v)


def _register_qe4():
    for op in dvo.OPS:
        if op.name == "QE4_ANT":
            return op
    spec = Spec(body=maxx(minn(Src0 - Src1, C0), C1))
    perf = {}
    try:
        low = lower(spec, ver="v3", perf_en=True, name="QE4_ANT")
        perf = {"v3": True}
    except Exception:
        low = lower(spec, ver="v3", perf_en=False, name="QE4_ANT")
    op = DveOp("QE4_ANT", spec, subdim=False,
               uops_sha={"v3": low.sha("v3")}, perf_en=perf,
               reference=_qe4_ref)
    dvo.OPS.append(op)
    dvo._SUB_OPCODE_FOR_NAME["QE4_ANT"] = dvo._CUSTOM_DVE_ROW_BASE + len(dvo.OPS) - 1
    dvo.CUSTOM_DVE_REFS["QE4_ANT"] = op.reference
    return op


QE4 = _register_qe4()


class Emit:

    def __init__(self, sems, reg, dry):
        self.sems, self.reg, self.dry = sems, reg, dry
        self.counts = {k: 0 for k in sems}
        self.eng = None

    def op(self, f, waits=(), inc=None, selfsync=False):
        if not self.dry:
            for w in waits:
                if w is None:
                    continue
                sem, val = self.reg[w]
                self.eng.wait_ge(self.sems[sem], val)
        instr = None if self.dry else f(self.eng)
        if selfsync and inc is None:
            inc = ("vy", 1, None)
        if inc is not None:
            sem, n, evs = inc
            self.counts[sem] += n
            if self.dry:
                if evs:
                    for ev in (evs if isinstance(evs, (list, tuple)) else [evs]):
                        self.reg[ev] = (sem, self.counts[sem])
            elif instr is not None:
                instr.then_inc(self.sems[sem], n)
            if selfsync and not self.dry:
                self.eng.wait_ge(self.sems[sem], self.counts[sem])
        return instr


def build_program(dbg=False):
    nc = bass.Bass("TRN2", target_bir_lowering=False,
                   detect_race_conditions=False)
    x_d = nc.dram_tensor("x", [SEQ * 2048, 1024], F32, kind="ExternalInput").ap()
    xt_d = nc.dram_tensor("xt", [SEQ * 1024, 2048], F32, kind="ExternalInput").ap()
    qp_d = nc.dram_tensor("qp", [1024, 256], F32, kind="ExternalInput").ap()
    kp_d = nc.dram_tensor("kp", [1024, 256], F32, kind="ExternalInput").ap()
    wt_d = nc.dram_tensor("wt", [1024, 1024], F32, kind="ExternalInput").ap()
    bias_d = nc.dram_tensor("bias", [1, 1024], F32, kind="ExternalInput").ap()
    si_d = nc.dram_tensor("si", [1, 1], F32, kind="ExternalInput").ap()
    sw_d = nc.dram_tensor("sw", [1, 1], F32, kind="ExternalInput").ap()
    out_d = nc.dram_tensor("out", [SEQ * 2048, 1024], F32, kind="ExternalOutput").ap()
    r_d = [nc.dram_tensor(f"rows_{s}", [4, 2048], F32, kind="Internal").ap()
           for s in range(SEQ)]

    from contextlib import ExitStack
    with ExitStack() as es:
        def sb(name, shape, dt=F32):
            return es.enter_context(nc.sbuf_tensor(name, shape, dt))

        wt8 = sb("wt8", [128, D8 * 1024], FP8)
        qp16 = sb("qp16", [128, D8 * 256], BF16)
        kp16 = sb("kp16", [128, D8 * 256], BF16)
        bias32 = sb("bias32", [128, 1024], F32)
        ident32 = sb("ident32", [128, 128], F32)
        identi = sb("identi", [128, 128], I32)
        ones16 = sb("ones16", [128, 1], BF16)
        ncq16 = sb("ncq16", [1, 256], BF16)
        nck16 = sb("nck16", [1, 256], BF16)
        scp = sb("scp", [128, 8], F32)

        xtm = sb("xtm", [128, 5 * 1024], F32)
        xfm = sb("xfm", [128, 4 * 512], F32)
        x16 = sb("x16", [128, 2 * D8 * 512], BF16)
        v1b = sb("v1b", [128, 2 * 512], BF16)
        x8 = sb("x8", [128, 2 * D8 * 2048], FP8)
        q16 = sb("q16", [128, 2 * 2048], BF16)
        k16 = sb("k16", [128, 4 * 256], BF16)
        xn16 = sb("xn16", [128, 2 * 4 * 1024], BF16)
        kvsb = sb("kvsb", [128, 2 * 1024], BF16)
        q2f = sb("q2f", [128, 2 * 512], F32)
        rsbc = sb("rsbc", [128, 2 * 512], F32)
        bcs32 = sb("bcs32", [128, 2 * 1024], F32)
        rscbc = sb("rscbc", [128, 2 * 512], BF16)
        mrscbc = sb("mrscbc", [128, 2 * 512], BF16)
        m16r = sb("m16r", [1, 2 * 512], BF16)
        st6 = sb("st6", [128, 156], F32)
        stt = sb("stt_", [128, 2 * 25], F32)
        spp = sb("spp", [128, 2 * 48], F32)
        st4 = sb("st4", [4, 2 * 512], F32)
        zsb = sb("zsb", [128, 2 * 32], F32)
        e1b = sb("e1b", [128, 1024], F32)
        outs = sb("outs", [128, 4 * 1024], F32)
        wsc = sb("wsc", [128, 4 * 1024], F32)

        pb = [es.enter_context(nc.psum_tensor(f"pb{i}", [128, 512], F32))
              for i in range(8)]

        semnames = ["x", "fm", "bc", "rw", "out", "sv", "ss", "sg", "stn", "vy", "sc"]
        sems = {n: es.enter_context(nc.semaphore(n)) for n in semnames}
        dbg_tensors = []
        dbg_outs = {}
        if dbg:
            for nm, tens in [("st6", st6), ("spp", spp), ("stt", stt), ("st4", st4),
                             ("m16r", m16r), ("rsbc", rsbc), ("rscbc", rscbc),
                             ("mrscbc", mrscbc), ("zsb", zsb), ("kvsb", kvsb),
                             ("q16", q16), ("k16", k16), ("xn16", xn16),
                             ("x16", x16), ("x8", x8),
                             ("scp", scp[:, 0:4]), ("ncq16", ncq16), ("nck16", nck16),
                             ("wt8", wt8), ("qp16", qp16), ("kp16", kp16),
                             ("bias32", bias32), ("ident32", ident32),
                             ("e1b", e1b), ("outs", outs)]:
                ap = tens[:]
                dbg_tensors.append((nm, ap))
                dt = ap.dtype
                dbg_outs[nm] = nc.dram_tensor(f"dbg_{nm}", list(ap.shape), dt,
                                              kind="ExternalOutput").ap()
            dps_kv = nc.dram_tensor("dps_kvsb", [128, 4 * 2048], BF16,
                                    kind="ExternalOutput").ap()
            dps_q16 = nc.dram_tensor("dps_q16", [128, 4 * 4096], BF16,
                                     kind="ExternalOutput").ap()
            dps_zsb = nc.dram_tensor("dps_zsb", [128, 4 * 32], F32,
                                     kind="ExternalOutput").ap()
            dps_x8 = nc.dram_tensor("dps_x8", [128, 4 * 16384], FP8,
                                    kind="ExternalOutput").ap()
        block = es.enter_context(nc.Block())
        reg = {}

        def xnxt(t):
            if t % 4 < 3:
                return f"x[{t + 1}]"
            return f"xdup[{t // 4}]"

        def fmnxt(k):
            g = k // 8
            if k % 8 < 7:
                return f"fm[{k + 1}]"
            if g == 0:
                return "fm0dup"
            if g <= 13:
                return f"x[{4 * g + 8}]"
            if g == 14:
                return "fm[120]"
            return "fmdup"

        def emit_sync(E):
            def x_batch(gg):
                for lc in range(LC):
                    t = 4 * gg + lc
                    w = [f"bn[{t - 5}]", f"xn[{t - 5}]"] if t >= 5 else []
                    E.op(lambda e, o=xtm[:, (t % 5) * 1024:(t % 5 + 1) * 1024],
                         i=x_d[t * 128:(t + 1) * 128, :]:
                         e.dma_start(out=o, in_=i),
                         waits=w, inc=("x", 16, f"x[{t}]"))
                t = 4 * gg + 3
                E.op(lambda e,
                     o=xtm[:, (t % 5) * 1024:(t % 5 + 1) * 1024],
                     i=x_d[t * 128:(t + 1) * 128, :]:
                     e.dma_start(out=o, in_=i),
                     inc=("x", 16, f"xdup[{gg}]"))

            def fm_batch(g):
                s, st = divmod(g, ST)
                for d in range(D8):
                    k = g * D8 + d
                    w = [f"c16[{k - 4}]"] if k >= 4 else []
                    src = xt_d[s * 1024 + d * 128:s * 1024 + (d + 1) * 128,
                               st * 512:(st + 1) * 512]
                    E.op(lambda e, o=xfm[:, (k % 4) * 512:(k % 4 + 1) * 512],
                         i=src: e.dma_start(out=o, in_=i),
                         waits=w, inc=("fm", 16, f"fm[{k}]"))

            x_batch(0)
            x_batch(1)
            fm_batch(0)
            E.op(lambda e: e.dma_start(
                out=xfm[:, 3 * 512:4 * 512],
                in_=xt_d[7 * 128:8 * 128, 0:512]), inc=("fm", 16, "fm0dup"))
            stages = ([("wt", d, wt_d[d * 128:(d + 1) * 128, :]) for d in range(D8)]
                      + [("qp", d, qp_d[d * 128:(d + 1) * 128, :]) for d in range(D8)]
                      + [("kp", d, kp_d[d * 128:(d + 1) * 128, :]) for d in range(D8)])
            for j, (_, _, src) in enumerate(stages):
                w = [f"wc[{j - 4}]"] if j >= 4 else []
                wslot = wsc[:, (j % 4) * 1024:(j % 4) * 1024 + src.shape[1]]
                E.op(lambda e, o=wslot, i=src: e.dma_start(out=o, in_=i),
                     waits=w, inc=("fm", 16, f"wl[{j}]"))
            _, _, lsrc = stages[-1]
            E.op(lambda e, o=wsc[:, 3 * 1024:3 * 1024 + lsrc.shape[1]], i=lsrc:
                 e.dma_start(out=o, in_=i), inc=("fm", 16, "wl[24]"))
            for gg in range(2, NG + 2):
                if gg >= 3:
                    fm_batch(gg - 2)
                if gg < NG:
                    x_batch(gg)
            E.op(lambda e: e.dma_start(
                out=xfm[:, (127 % 4) * 512:(127 % 4 + 1) * 512],
                in_=xt_d[3 * 1024 + 7 * 128:3 * 1024 + 8 * 128,
                         3 * 512:4 * 512]), inc=("fm", 16, "fmdup"))

        def sc_casts(E, g):
            for d in range(D8):
                k = g * D8 + d
                w = [fmnxt(k)]
                if g >= 2:
                    w += [f"Adone[{g - 2}]", f"qnt[{(g - 2) * D8 + d}]"]
                dst = x16[:, ((g % 2) * D8 + d) * 512:((g % 2) * D8 + d + 1) * 512]
                E.op(lambda e, o=dst,
                     i=xfm[:, (k % 4) * 512:(k % 4 + 1) * 512]:
                     e.activation(out=o, in_=i, func=AF.Copy),
                     waits=w, inc=("sc", 1, f"c16[{k}]"))

        def gp_rows(E, g):
            s, st = divmod(g, ST)
            E.op(lambda e, o=r_d[s][:, st * 512:(st + 1) * 512],
                 i=st4[:, (g % 2) * 512:(g % 2 + 1) * 512]:
                 e.dma_start(out=o, in_=i),
                 waits=[f"rc[{g}]"], inc=("rw", 16, None))
            E.op(lambda e, o=r_d[s][:, st * 512:(st + 1) * 512],
                 i=st4[:, (g % 2) * 512:(g % 2 + 1) * 512]:
                 e.dma_start(out=o, in_=i), inc=("rw", 16, f"rows[{g}]"))

        def gp_xo(E, t, ev):
            E.op(lambda e, o=outs[:, (t % 4) * 1024:(t % 4 + 1) * 1024],
                 i=x_d[t * 128:(t + 1) * 128, :]:
                 e.dma_start(out=o, in_=i), inc=("out", 16, ev))

        def emit_gpsimd(E):
            for t0 in range(4):
                gp_xo(E, t0, f"xo[{t0}]")
            E.op(lambda e: e.iota(identi[:], pattern=[[1, 128]], base=0,
                                  channel_multiplier=-1))
            E.op(lambda e: e.tensor_single_scalar(out=identi[:], in_=identi[:],
                                                  scalar=0, op=ALU.is_equal))
            E.op(lambda e: e.tensor_copy(out=ident32[:], in_=identi[:]),
                 inc=("sg", 1, "ident"))
            gp_rows(E, 0)
            for s in range(SEQ):
                for st in range(ST):
                    g = 4 * s + st
                    if g + 1 < NG and (g + 1) % 4 != 0:
                        gp_rows(E, g + 1)
                for st in range(ST):
                    for lc in range(LC):
                        t = (4 * s + st) * 4 + lc
                        oslot = outs[:, (t % 4) * 1024:(t % 4 + 1) * 1024]
                        E.op(lambda e, o=oslot: e.tensor_tensor(
                            out=o, in0=o, in1=bias32[:], op=ALU.add),
                             waits=[f"e2[{t}]", "swbc" if t == 0 else None],
                             inc=("sg", 1, f"e3[{t}]"))
                        E.op(lambda e, o=out_d[t * 128:(t + 1) * 128, :], i=oslot:
                             e.dma_start(out=o, in_=i),
                             inc=("out", 16, f"out[{t}]"))
                        if t + 4 < NT:
                            gp_xo(E, t + 4, f"xo[{t + 4}]")
                        elif t == NT - 4:
                            E.op(lambda e, o=outs[:, 0:1024],
                                 i=x_d[(NT - 1) * 128:NT * 128, :]:
                                 e.dma_start(out=o, in_=i),
                                 inc=("out", 16, "xod"))
                if 4 * (s + 1) < NG:
                    gp_rows(E, 4 * (s + 1))
                if dbg:
                    E.op(lambda e, s_=s: e.dma_start(
                        out=dps_kv[:, s_ * 2048:(s_ + 1) * 2048], in_=kvsb[:]),
                         waits=[f"kvev[{s}]"], inc=("rw", 16, f"dkv[{s}]"))
                    E.op(lambda e, s_=s: e.dma_start(
                        out=dps_q16[:, s_ * 4096:(s_ + 1) * 4096], in_=q16[:]),
                         waits=[f"e1[{16 * s + 15}]"],
                         inc=("rw", 16, f"dq16[{s}]"))
                    E.op(lambda e, s_=s: e.dma_start(
                        out=dps_zsb[:, s_ * 32:(s_ + 1) * 32],
                        in_=zsb[:, (s_ % 2) * 32:(s_ % 2) * 32 + 32]),
                         waits=[f"z[{4 * s + 3}]"], inc=("rw", 16, f"dzs[{s}]"))
                    E.op(lambda e, s_=s: e.dma_start(
                        out=dps_x8[:, s_ * 16384:(s_ + 1) * 16384],
                        in_=x8[:, (s_ % 2) * 16384:(s_ % 2) * 16384 + 16384]),
                         waits=[f"qntall[{s}]"], inc=("rw", 16, f"dx8[{s}]"))
            if dbg:
                for nm, ap in dbg_tensors:
                    E.op(lambda e, o=dbg_outs[nm], i=ap: e.dma_start(out=o, in_=i),
                         waits=[f"out[{NT - 1}]"], inc=("rw", 16, f"dbg[{nm}]"))
                E.op(lambda e: e.nop(),
                     waits=[f"dbg[{dbg_tensors[-1][0]}]", "dkv[3]",
                            "dq16[3]", "dzs[3]", "dx8[3]"])
            E.op(lambda e: e.nop(), waits=[f"out[{NT - 1}]"])

        def sc_arrival(E, g):
            for lc in range(LC):
                t = 4 * g + lc
                w = [xnxt(t), f"sta[{g}]"]
                if g >= 2:
                    w.append(f"kv[{g - 2},{lc}]")
                xslot = xtm[:, (t % 5) * 1024:(t % 5 + 1) * 1024]
                xnslot = xn16[:, ((g % 2) * 4 + lc) * 1024:
                              ((g % 2) * 4 + lc + 1) * 1024]
                rs_ap = spp[:, (g % 2) * 48 + 8 + lc:(g % 2) * 48 + 9 + lc]
                nm_ap = spp[:, (g % 2) * 48 + 16 + lc:(g % 2) * 48 + 17 + lc]
                E.op(lambda e, o=xnslot, i=xslot, sc=rs_ap, bi=nm_ap:
                     e.activation(out=o, in_=i, func=AF.Identity, scale=sc, bias=bi),
                     waits=w, inc=("ss", 1, f"xn[{t}]"))

        def sc_bc(E, g):
            s, st = divmod(g, ST)
            w = [f"rows[{g}]"]
            if g >= 2:
                w += [f"q2[{g - 2},1]", f"bcc[{g - 2}]"]
            E.op(lambda e, o=rsbc[:, (g % 2) * 512:(g % 2 + 1) * 512],
                 i=r_d[s][1:2, st * 512:(st + 1) * 512].broadcast_to([128, 512]):
                 e.dma_start(out=o, in_=i), waits=w, inc=("bc", 16, None))
            E.op(lambda e, o=bcs32[:, (g % 2) * 1024:(g % 2) * 1024 + 512],
                 i=r_d[s][2:3, st * 512:(st + 1) * 512].broadcast_to([128, 512]):
                 e.dma_start(out=o, in_=i), inc=("bc", 16, None))
            E.op(lambda e, o=bcs32[:, (g % 2) * 1024 + 512:(g % 2) * 1024 + 1024],
                 i=r_d[s][3:4, st * 512:(st + 1) * 512].broadcast_to([128, 512]):
                 e.dma_start(out=o, in_=i), inc=("bc", 16, None))
            E.op(lambda e, o=bcs32[:, (g % 2) * 1024 + 512:(g % 2) * 1024 + 1024],
                 i=r_d[s][3:4, st * 512:(st + 1) * 512].broadcast_to([128, 512]):
                 e.dma_start(out=o, in_=i), inc=("bc", 16, f"bc[{g}]"))

        def emit_scalar(E):
            E.op(lambda e: e.dma_start(out=bias32[:],
                                       in_=bias_d.broadcast_to([128, 1024])),
                 inc=("bc", 16, "biasbc"))
            E.op(lambda e: e.dma_start(out=scp[:, 0:1],
                                       in_=si_d.broadcast_to([128, 1])),
                 inc=("bc", 16, "sibc"))
            E.op(lambda e: e.dma_start(out=scp[:, 1:2],
                                       in_=sw_d.broadcast_to([128, 1])),
                 inc=("bc", 16, "swbc"))
            E.op(lambda e: e.dma_start(out=scp[:, 1:2],
                                       in_=sw_d.broadcast_to([128, 1])),
                 inc=("bc", 16, "swg"))
            sc_arrival(E, 0)
            sc_arrival(E, 1)
            sc_casts(E, 0)
            sc_casts(E, 1)
            sc_bc(E, 0)
            for s in range(SEQ):
                for st in range(ST):
                    g = 4 * s + st
                    for lc in range(LC):
                        j = 4 * g + lc
                        w = [f"kpre[{g},{lc // 2}]"]
                        if j >= 4:
                            w.append(f"kv[{(j - 4) // 4},{(j - 4) % 4}]")
                        src = (pb[6] if lc < 2 else pb[7])[:, (lc % 2) * 256:
                                                           (lc % 2 + 1) * 256]
                        kslot = k16[:, (j % 4) * 256:(j % 4 + 1) * 256]
                        rs_ap = spp[:, (g % 2) * 48 + 8 + lc:(g % 2) * 48 + 9 + lc]
                        E.op(lambda e, o=kslot, i=src, sc=rs_ap:
                             e.activation(out=o, in_=i, func=AF.Exp, scale=sc),
                             waits=w, inc=("ss", 1, f"kexp[{g},{lc}]"))
                    for ee in range(2):
                        E.op(lambda e, o=q16[:, ee * 2048 + st * 512:
                                             ee * 2048 + (st + 1) * 512],
                             i=q2f[:, ee * 512:(ee + 1) * 512]:
                             e.activation(out=o, in_=i, func=AF.Exp),
                             waits=[f"q2[{g},{ee}]"],
                             inc=("ss", 1, f"qexp[{g},{ee}]"))
                    if st == ST - 1:
                        for ee in range(2):
                            for dh in range(2):
                                E.op(lambda e, o=kvsb[:, ee * 1024 + dh * 512:
                                                      ee * 1024 + (dh + 1) * 512],
                                     i=pb[2 * ee + dh][:]:
                                     e.activation(out=o, in_=i, func=AF.Copy),
                                     waits=([f"kv[{g},3]"]
                                            + ([f"dkv[{s - 1}]"]
                                               if (dbg and s >= 1) else []))
                                     if (ee == 0 and dh == 0) else [],
                                     inc=("ss", 1, f"kvev[{s}]"
                                          if (ee == 1 and dh == 1) else None))
                    if g + 2 < NG and st < 2:
                        sc_arrival(E, g + 2)
                        sc_casts(E, g + 2)
                    if g + 1 < NG and (g + 1) % 4 != 0:
                        sc_bc(E, g + 1)
                for st in range(ST):
                    if st < 2 and 4 * (s + 1) + st < NG:
                        sc_arrival(E, 4 * (s + 1) + st)
                        sc_casts(E, 4 * (s + 1) + st)
                if 4 * (s + 1) < NG:
                    sc_bc(E, 4 * (s + 1))

        def vc_stats(E, g):
            b = (g % 2) * 48
            sp_ = (g % 2) * 48
            w0 = []
            if g >= 2:
                w0 = [f"kexp[{g - 2},3]", f"Tst[{g - 2}]"]
            for lc in range(LC):
                t = 4 * g + lc
                xslot = xtm[:, (t % 5) * 1024:(t % 5 + 1) * 1024]
                o6 = st6[:, b + lc * 12:b + lc * 12 + 12]
                E.op(lambda e, o=o6, i=xslot:
                     e.bn_stats(out=o[:, 0:6], in_=i[:, 0:512]),
                     waits=[xnxt(t)] + (w0 if lc == 0 else []))
                E.op(lambda e, o=o6, i=xslot:
                     e.bn_stats(out=o[:, 6:12], in_=i[:, 512:1024]),
                     inc=("sv", 1, f"bn[{t}]"))
            E.op(lambda e: e.bn_stats(out=st6[:, 144:150],
                                      in_=xtm[:, (4 * g + 3) % 5 * 1024:
                                              (4 * g + 3) % 5 * 1024 + 64]))
            E.op(lambda e: e.bn_stats(out=st6[:, 150:156],
                                      in_=xtm[:, (4 * g + 3) % 5 * 1024:
                                              (4 * g + 3) % 5 * 1024 + 64]))
            v3 = st6[:, b:b + 96].rearrange("p (l q k) -> p l q k", q=4, k=3)
            mg = v3[:, :, :, 1]
            cvg = v3[:, :, :, 2]
            cnt = v3[:, :, :, 0]
            sm = spp[:, sp_ + 0:sp_ + 8]
            rs = spp[:, sp_ + 8:sp_ + 16]
            nmrs = spp[:, sp_ + 16:sp_ + 24]
            ta = spp[:, sp_ + 24:sp_ + 32]
            tb = spp[:, sp_ + 32:sp_ + 40]
            tc_ = spp[:, sp_ + 40:sp_ + 48]
            stv = stt[:, (g % 2) * 25:(g % 2) * 25 + 25]
            E.op(lambda e: e.tensor_tensor(out=cnt, in0=mg, in1=mg, op=ALU.mult),
                 selfsync=True)
            E.op(lambda e: e.tensor_reduce(out=sm, in_=mg, axis=AX.X, op=ALU.add),
                 selfsync=True)
            E.op(lambda e: e.tensor_reduce(out=tb, in_=cvg, axis=AX.X, op=ALU.add),
                 selfsync=True)
            E.op(lambda e: e.tensor_reduce(out=tc_, in_=cnt, axis=AX.X, op=ALU.add),
                 selfsync=True)
            E.op(lambda e: e.tensor_copy(out=stv[:, 0:25:5], in_=sm[:, 0:5]),
                 selfsync=True)
            E.op(lambda e: e.tensor_tensor(out=ta, in0=sm, in1=sm, op=ALU.mult),
                 selfsync=True)
            E.op(lambda e: e.tensor_single_scalar(out=tb, in_=tb,
                                                  scalar=1.0 / 1024.0, op=ALU.mult),
                 selfsync=True)
            E.op(lambda e: e.tensor_single_scalar(out=tc_, in_=tc_,
                                                  scalar=0.25, op=ALU.mult),
                 selfsync=True)
            E.op(lambda e: e.tensor_scalar(out=ta, in0=ta, scalar1=-1.0 / 16.0,
                                           scalar2=1e-5, op0=ALU.mult, op1=ALU.add),
                 selfsync=True)
            E.op(lambda e: e.tensor_tensor(out=tb, in0=tb, in1=tc_, op=ALU.add),
                 selfsync=True)
            E.op(lambda e: e.tensor_tensor(out=ta, in0=ta, in1=tb, op=ALU.add),
                 selfsync=True)
            E.op(lambda e: e.tensor_single_scalar(out=tb, in_=ta,
                                                  scalar=0.5, op=ALU.mult),
                 selfsync=True)
            E.op(lambda e: e.tensor_scalar(out=rs, in0=ta, scalar1=-0.5,
                                           scalar2=1.5, op0=ALU.mult, op1=ALU.add),
                 selfsync=True)
            for _ in range(3):
                E.op(lambda e: e.tensor_tensor(out=tc_, in0=rs, in1=rs,
                                               op=ALU.mult), selfsync=True)
                E.op(lambda e: e.tensor_tensor(out=tc_, in0=tb, in1=tc_,
                                               op=ALU.mult), selfsync=True)
                E.op(lambda e: e.tensor_scalar(out=tc_, in0=tc_, scalar1=-1.0,
                                               scalar2=1.5, op0=ALU.mult,
                                               op1=ALU.add), selfsync=True)
                E.op(lambda e: e.tensor_tensor(out=rs, in0=rs, in1=tc_,
                                               op=ALU.mult), selfsync=True)
            E.op(lambda e: e.scalar_tensor_tensor(out=nmrs, in0=sm, scalar=-0.25,
                                                  in1=rs, op0=ALU.mult, op1=ALU.mult),
                 selfsync=True)
            E.op(lambda e: e.tensor_copy(out=stv[:, 1:25:5], in_=rs[:, 0:5]),
                 selfsync=True)
            rscv = spp[:, sp_ + 24:sp_ + 32]
            E.op(lambda e: e.tensor_scalar(out=rscv, in0=rs, scalar1=scp[:, 2:3],
                                           scalar2=None, op0=ALU.mult),
                 waits=(["scprep"] if g == 0 else []), selfsync=True)
            E.op(lambda e: e.tensor_copy(out=stv[:, 2:25:5], in_=rscv[:, 0:5]),
                 selfsync=True)
            E.op(lambda e: e.scalar_tensor_tensor(out=stv[:, 3:25:5],
                                                  in0=sm[:, 0:5],
                                                  scalar=0.25, in1=rscv[:, 0:5],
                                                  op0=ALU.mult, op1=ALU.mult),
                 inc=("sv", 1, f"sta[{g}]"), selfsync=True)

        def vc_rowcopy(E, g):
            w = [f"Tst[{g}]"]
            if g >= 2:
                w += [f"rows[{g - 2}]", f"Adone[{g - 2}]"]
            E.op(lambda e, o=st4[:, (g % 2) * 512:(g % 2 + 1) * 512]:
                 e.tensor_copy(out=o, in_=pb[6][0:4, :]), waits=w, selfsync=True)
            E.op(lambda e, o=m16r[0:1, (g % 2) * 512:(g % 2 + 1) * 512],
                 i=st4[0:1, (g % 2) * 512:(g % 2 + 1) * 512]:
                 e.tensor_copy(out=o, in_=i),
                 inc=("sv", 1, f"rc[{g}]"), selfsync=True)

        def vc_bccast(E, g):
            E.op(lambda e, o=rscbc[:, (g % 2) * 512:(g % 2 + 1) * 512],
                 i=bcs32[:, (g % 2) * 1024:(g % 2) * 1024 + 512]:
                 e.tensor_copy(out=o, in_=i), waits=[f"bc[{g}]"])
            E.op(lambda e, o=mrscbc[:, (g % 2) * 512:(g % 2 + 1) * 512],
                 i=bcs32[:, (g % 2) * 1024 + 512:(g % 2) * 1024 + 1024]:
                 e.tensor_copy(out=o, in_=i),
                 inc=("sv", 1, f"bcc[{g}]"))

        def vc_q2(E, g):
            for ee in range(2):
                w = [f"qpre[{g},{ee}]", f"bc[{g}]"]
                if g >= 1:
                    w.append(f"qexp[{g - 1},{ee}]")
                E.op(lambda e, o=q2f[:, ee * 512:(ee + 1) * 512], b=4 + ee,
                     i1=rsbc[:, (g % 2) * 512:(g % 2 + 1) * 512]:
                     e.tensor_tensor(out=o, in0=pb[b][:], in1=i1, op=ALU.mult),
                     waits=w, inc=("sv", 1, f"q2[{g},{ee}]"))

        def vc_quant(E, g):
            s, st = divmod(g, ST)
            for d in range(D8):
                k = g * D8 + d
                w = [f"c16[{k}]", f"bcc[{g}]"]
                if s >= 2 and d == 0 and st == 0:
                    w.append(f"ffn[{(4 * (s - 2) + 3) * 4 + 3}]")
                    if dbg:
                        w.append(f"dx8[{s - 2}]")
                xs = x16[:, ((g % 2) * D8 + d) * 512:((g % 2) * D8 + d + 1) * 512]
                vs = v1b[:, (d % 2) * 512:(d % 2 + 1) * 512]
                E.op(lambda e, o=vs, i0=xs,
                     i1=rscbc[:, (g % 2) * 512:(g % 2 + 1) * 512]:
                     e.tensor_tensor(out=o, in0=i0, in1=i1, op=ALU.mult), waits=w)
                x8s = x8[:, ((s % 2) * D8 + d) * 2048 + st * 512:
                         ((s % 2) * D8 + d) * 2048 + (st + 1) * 512]
                evs = [f"qnt[{k}]"]
                if st == ST - 1 and d == D8 - 1:
                    evs.append(f"qntall[{s}]")
                E.op(lambda e, o=x8s, i0=vs,
                     i1=mrscbc[:, (g % 2) * 512:(g % 2 + 1) * 512]:
                     e._custom_dve(QE4, out=o, in0=i0, in1=i1, s0=240.0,
                                   s1=-240.0),
                     inc=("sv", 1, evs))

        def vc_z(E, g):
            s, st = divmod(g, ST)
            w = [f"den[{g}]"]
            if s >= 2 and st == 0:
                w.append(f"e1[{(s - 2) * 16 + 15}]")
                if dbg:
                    w.append(f"dzs[{s - 2}]")
            zv = zsb[:, (s % 2) * 32 + st * 8:(s % 2) * 32 + st * 8 + 8]
            E.op(lambda e, o=zv: e.tensor_single_scalar(
                out=o, in_=pb[4][:, 0:8], scalar=1e-8, op=ALU.add), waits=w,
                 selfsync=True)
            E.op(lambda e, o=zv: e.reciprocal(out=o, in_=o),
                 inc=("sv", 1, f"z[{g}]"), selfsync=True)

        def vc_epi(E, t):
            s = t // 16
            w = [f"num[{t}]", f"z[{(t // 4)}]",
                 f"xo[{t + 1}]" if t + 1 < NT else "xod"]
            z_ap = zsb[:, (s % 2) * 32 + (t % 16) // 4 * 8 + t % 4:
                       (s % 2) * 32 + (t % 16) // 4 * 8 + t % 4 + 1]
            xb = outs[:, (t % 4) * 1024:(t % 4 + 1) * 1024]
            nb = 4 if t % 2 == 0 else 0
            fb = 6 if t % 2 == 0 else 2
            E.op(lambda e, za=z_ap, xv=xb, b=nb: e.scalar_tensor_tensor(
                out=e1b[:, 0:512], in0=pb[b][:], scalar=za, in1=xv[:, 0:512],
                op0=ALU.mult, op1=ALU.add), waits=w)
            E.op(lambda e, za=z_ap, xv=xb, b=nb: e.scalar_tensor_tensor(
                out=e1b[:, 512:1024], in0=pb[b + 1][:], scalar=za,
                in1=xv[:, 512:1024],
                op0=ALU.mult, op1=ALU.add), inc=("sv", 1, f"e1[{t}]"))
            w2 = [f"ffn[{t}]"]
            if t >= 4:
                w2.append(f"out[{t - 4}]")
            oslot = outs[:, (t % 4) * 1024:(t % 4 + 1) * 1024]
            E.op(lambda e, o=oslot, b=fb: e.scalar_tensor_tensor(
                out=o[:, 0:512], in0=pb[b][:], scalar=scp[:, 3:4],
                in1=e1b[:, 0:512], op0=ALU.mult, op1=ALU.add), waits=w2)
            E.op(lambda e, o=oslot, b=fb: e.scalar_tensor_tensor(
                out=o[:, 512:1024], in0=pb[b + 1][:], scalar=scp[:, 3:4],
                in1=e1b[:, 512:1024], op0=ALU.mult, op1=ALU.add),
                 inc=("sv", 1, f"e2[{t}]"))

        def emit_vector(E):
            E.op(lambda e: e.memset(ones16[:], 1.0), inc=("sv", 1, "ones"))
            E.op(lambda e: e.memset(st6[:], 0.0))
            if dbg:
                for tns in (stt, spp, zsb, st4, m16r, rsbc, e1b, outs,
                            bcs32, rscbc, mrscbc, q2f):
                    E.op(lambda e, tt=tns: e.memset(tt[:], 0.0))
            E.op(lambda e: e.reciprocal(out=scp[:, 2:3], in_=scp[:, 0:1]),
                 waits=["swbc"])
            E.op(lambda e: e.tensor_tensor(out=scp[:, 3:4], in0=scp[:, 0:1],
                                           in1=scp[:, 1:2], op=ALU.mult),
                 waits=["swg"], inc=("sv", 1, "scprep"))
            vc_stats(E, 0)
            vc_stats(E, 1)
            for j in range(24):
                kind = "wt" if j < 8 else ("qp" if j < 16 else "kp")
                d = j % 8
                n = 1024 if kind == "wt" else 256
                wslot = wsc[:, (j % 4) * 1024:(j % 4) * 1024 + n]
                dst = (wt8[:, d * 1024:(d + 1) * 1024] if kind == "wt" else
                       (qp16 if kind == "qp" else kp16)[:, d * 256:(d + 1) * 256])
                E.op(lambda e, o=dst, i=wslot: e.tensor_copy(out=o, in_=i),
                     waits=[f"wl[{j + 1}]"], inc=("sv", 1, f"wc[{j}]"))
            E.op(lambda e: e.tensor_single_scalar(out=ncq16[:], in_=pb[4][0:1, 0:256],
                                                  scalar=-0.25, op=ALU.mult),
                 waits=["colq"], inc=("sv", 1, "ncq"))
            E.op(lambda e: e.tensor_single_scalar(out=nck16[:], in_=pb[5][0:1, 0:256],
                                                  scalar=-0.25, op=ALU.mult),
                 waits=["colk"], inc=("sv", 1, "nck"))
            for s in range(SEQ):
                for st in range(ST):
                    g = 4 * s + st
                    if st > 0:
                        vc_z(E, g - 1)
                    vc_rowcopy(E, g)
                    if g + 2 < NG and st < 2:
                        vc_stats(E, g + 2)
                    vc_q2(E, g)
                    vc_bccast(E, g)
                    vc_quant(E, g)
                vc_z(E, 4 * s + 3)
                for k2 in range(2):
                    if 4 * (s + 1) + k2 < NG:
                        vc_stats(E, 4 * (s + 1) + k2)
                for st in range(ST):
                    for lc in range(LC):
                        vc_epi(E, (4 * s + st) * 4 + lc)

        def tn_statsT(E, g):
            w = [f"sta[{g}]"]
            if g == 0:
                w.append("ident")
            if g % 4 == 0 and g > 0:
                w.append(f"e2[{(g // 4) * 16 - 2}]")
            elif g > 0:
                w += [f"kexp[{g - 1},0]", f"kexp[{g - 1},1]"]
            for lc in range(LC):
                E.op(lambda e, l=lc, sv=stt[:, (g % 2) * 25 + 5 * lc:
                                            (g % 2) * 25 + 5 * lc + 4]:
                     e.transpose(pb[6][0:4, l * 128:(l + 1) * 128], sv, ident32[:]),
                     waits=(w if lc == 0 else []),
                     inc=(("stn", 1, f"Tst[{g}]") if lc == LC - 1 else None))

        def tn_kpre(E, g):
            s, st = divmod(g, ST)
            w = [f"c16[{g * D8 + D8 - 1}]", f"rc[{g}]"]
            if g == 0:
                w.append("nck")
            if g >= 1:
                w += [f"kexp[{g - 1},2]", f"kexp[{g - 1},3]"]
            if st == 0 and s >= 1:
                w.append(f"e2[{s * 16 - 1}]")
            for pair in range(2):
                bank = 6 + pair
                for li in range(2):
                    lc = 2 * pair + li
                    for d in range(D8):
                        lhsT = x16[:, ((g % 2) * D8 + d) * 512 + lc * 128:
                                   ((g % 2) * D8 + d) * 512 + (lc + 1) * 128]
                        E.op(lambda e, l=lhsT, dd=d, b=bank, li_=li: e.matmul(
                            pb[b][:, li_ * 256:(li_ + 1) * 256], l,
                            kp16[:, dd * 256:(dd + 1) * 256],
                            start=(dd == 0), stop=False),
                             waits=(w if (pair == 0 and li == 0 and d == 0) else []))
                    E.op(lambda e, b=bank, li_=li, lcv=lc: e.matmul(
                        pb[b][:, li_ * 256:(li_ + 1) * 256],
                        m16r[0:1, (g % 2) * 512 + lcv * 128:
                             (g % 2) * 512 + (lcv + 1) * 128],
                        nck16[:], start=False, stop=True),
                         inc=("stn", 1, f"kpre[{g},{pair}]"))

        def tn_qpre(E, g):
            s, st = divmod(g, ST)
            for ee in range(2):
                w = []
                if ee == 0:
                    w.append(f"c16[{g * D8 + D8 - 1}]")
                    if g >= 1:
                        w.append(f"z[{g - 1}]")
                else:
                    if g >= 1:
                        w.append(f"q2[{g - 1},1]")
                if dbg and st == 0 and ee == 0 and s >= 1:
                    w.append(f"dq16[{s - 1}]")
                if st == 0 and s >= 1:
                    w.append(f"e1[{s * 16 - 1}]")
                for d in range(D8):
                    lhsT = qp16[:, d * 256 + ee * 128:d * 256 + (ee + 1) * 128]
                    rhs = x16[:, ((g % 2) * D8 + d) * 512:
                              ((g % 2) * D8 + d + 1) * 512]
                    E.op(lambda e, l=lhsT, r=rhs, b=4 + ee, dd=d: e.matmul(
                        pb[b][:], l, r, start=(dd == 0), stop=False),
                         waits=(w if d == 0 else []))
                cw = [f"rc[{g}]"]
                if g == 0 and ee == 0:
                    cw += ["ncq", "nck"]
                E.op(lambda e, b=4 + ee, eev=ee: e.matmul(
                    pb[b][:], ncq16[0:1, eev * 128:(eev + 1) * 128],
                    m16r[0:1, (g % 2) * 512:(g % 2 + 1) * 512],
                    start=False, stop=True), waits=cw,
                     inc=("stn", 1, f"qpre[{g},{ee}]"))

        def tn_kv(E, g):
            s, st = divmod(g, ST)
            for lc in range(LC):
                t = 4 * g + lc
                w = [f"kexp[{g},{lc}]", f"xn[{t}]"]
                if st == 0 and lc == 0 and s >= 1:
                    w.append(f"kvev[{s - 1}]")
                first = (st == 0 and lc == 0)
                last = (st == ST - 1 and lc == LC - 1)
                n = 0
                for ee in range(2):
                    for dh in range(2):
                        j = 4 * g + lc
                        lhsT = k16[:, (j % 4) * 256 + ee * 128:
                                   (j % 4) * 256 + (ee + 1) * 128]
                        rhs = xn16[:, ((g % 2) * 4 + lc) * 1024 + dh * 512:
                                   ((g % 2) * 4 + lc) * 1024 + (dh + 1) * 512]
                        n += 1
                        E.op(lambda e, l=lhsT, r=rhs, b=2 * ee + dh: e.matmul(
                            pb[b][:], l, r, start=first, stop=last),
                             waits=(w if n == 1 else []),
                             inc=(("stn", 1, f"kv[{g},{lc}]")
                                  if n == 4 else None))

        def tn_den(E, g):
            s, st = divmod(g, ST)
            w = [f"qexp[{g},0]", f"qexp[{g},1]", f"q2[{g},0]", "ones"]
            n = 0
            for lc in range(LC):
                for ee in range(2):
                    lhsT = q16[:, ee * 2048 + st * 512 + lc * 128:
                               ee * 2048 + st * 512 + (lc + 1) * 128]
                    n += 1
                    E.op(lambda e, l=lhsT, lcv=lc, eev=ee: e.matmul(
                        pb[4][:, lcv:lcv + 1], l, ones16[:],
                        start=(eev == 0), stop=(eev == 1)),
                         waits=(w if n == 1 else []))
            E.op(lambda e: e.matmul(pb[4][:, 4:8], qp16[:, 0:128],
                                    qp16[:, 0:4], start=True, stop=True),
                 inc=("stn", 1, [f"den[{g}]", f"Adone[{g}]"]))

        def tn_num(E, t):
            s = t // 16
            g = t // 4
            st = (t % 16) // 4
            lc = t % 4
            nb = 4 if t % 2 == 0 else 0
            w = [f"kvev[{s}]", f"qexp[{g},0]", f"qexp[{g},1]"]
            if t % 16 == 0:
                w.append(f"z[{4 * s + 3}]")
            elif t % 16 >= 2:
                w.append(f"e1[{t - 2}]")
            n = 0
            for ee in range(2):
                for dh in range(2):
                    lhsT = q16[:, ee * 2048 + st * 512 + lc * 128:
                               ee * 2048 + st * 512 + (lc + 1) * 128]
                    rhs = kvsb[:, ee * 1024 + dh * 512:ee * 1024 + (dh + 1) * 512]
                    n += 1
                    E.op(lambda e, l=lhsT, r=rhs, b=nb + dh, eev=ee: e.matmul(
                        pb[b][:], l, r, start=(eev == 0), stop=(eev == 1)),
                         waits=(w if n == 1 else []),
                         inc=(("stn", 1, f"num[{t}]") if n == 4 else None))

        def tn_ffn(E, t):
            s = t // 16
            g = t // 4
            st = (t % 16) // 4
            lc = t % 4
            fb = 6 if t % 2 == 0 else 2
            w = [f"qnt[{g * D8 + D8 - 1}]"]
            if t % 16 == 0:
                w += [f"kexp[{4 * s + 3},0]", f"kexp[{4 * s + 3},1]",
                      f"kexp[{4 * s + 3},2]", f"kexp[{4 * s + 3},3]"]
            elif t % 16 == 1:
                w.append(f"kvev[{s}]")
            else:
                w.append(f"e2[{t - 2}]")
            n = 0
            off = st * 512 + lc * 128
            for e4 in range(4):
                lp = x8[:, ((s % 2) * D8 + 2 * e4) * 2048:
                        ((s % 2) * D8 + 2 * e4 + 2) * 2048].rearrange(
                    "p (d c) -> p d c", d=2)[:, :, off:off + 128]
                for oh in range(2):
                    rp = wt8[:, 2 * e4 * 1024:(2 * e4 + 2) * 1024].rearrange(
                        "p (d c) -> p d c", d=2)[:, :, oh * 512:(oh + 1) * 512]
                    n += 1
                    E.op(lambda e, l=lp, r=rp, b=fb + oh, ee4=e4: e.matmul(
                        pb[b][:], l, r, start=(ee4 == 0), stop=(ee4 == 3),
                        perf_mode=mybir.MatmulPerfMode.DoubleRow),
                         waits=(w if n == 1 else []),
                         inc=(("stn", 1, f"ffn[{t}]") if n == 8 else None))

        def emit_tensor(E):
            for ee, (proj, pbk, ev) in enumerate([(qp16, 4, "colq"),
                                                  (kp16, 5, "colk")]):
                for d in range(D8):
                    w = [f"wc[{8 + 8 * ee + d}]"]
                    if ee == 0 and d == 0:
                        w.append("ones")
                    E.op(lambda e, p=proj, b=pbk, dd=d: e.matmul(
                        pb[b][0:1, 0:256], ones16[:],
                        p[:, dd * 256:(dd + 1) * 256],
                        start=(dd == 0), stop=(dd == D8 - 1)),
                         waits=w,
                         inc=(("stn", 1, ev) if d == D8 - 1 else None))
            tn_statsT(E, 0)
            for s in range(SEQ):
                for st in range(ST):
                    g = 4 * s + st
                    if g % 4 == 0 and g > 0:
                        tn_statsT(E, g)
                    if st > 0:
                        tn_den(E, g - 1)
                    tn_kpre(E, g)
                    tn_qpre(E, g)
                    if g + 1 < NG and (g + 1) % 4 != 0:
                        tn_statsT(E, g + 1)
                    tn_kv(E, g)
                    if st == 3:
                        tn_den(E, g)
                for st in range(ST):
                    for lc in range(LC):
                        t = (4 * s + st) * 4 + lc
                        tn_ffn(E, t)
                        tn_num(E, t)

        E0 = Emit(sems, reg, dry=True)
        emit_sync(E0)
        emit_gpsimd(E0)
        emit_scalar(E0)
        emit_vector(E0)
        emit_tensor(E0)

        E1 = Emit(sems, reg, dry=False)

        @block.sync
        def _(eng):
            E1.eng = eng
            emit_sync(E1)

        @block.gpsimd
        def _(eng):
            E1.eng = eng
            emit_gpsimd(E1)

        @block.scalar
        def _(eng):
            E1.eng = eng
            emit_scalar(E1)

        @block.vector
        def _(eng):
            E1.eng = eng
            emit_vector(E1)

        @block.tensor
        def _(eng):
            E1.eng = eng
            emit_tensor(E1)

    mybir.codegen_inst_isa_subclasses(nc)
    build_program._last_reg = reg
    return nc


_NC_CACHE = {}


def kernel(L0_attn_x, L0_attn_q_proj, L0_attn_k_proj, L0_ffn_weight_fp8,
           L0_ffn_bias, L0_ffn_input_scale, L0_ffn_weight_scale):
    x = np.ascontiguousarray(np.asarray(L0_attn_x, dtype=np.float32))
    qp = np.ascontiguousarray(np.asarray(L0_attn_q_proj, dtype=np.float32))
    kp = np.ascontiguousarray(np.asarray(L0_attn_k_proj, dtype=np.float32))
    w = np.ascontiguousarray(np.asarray(L0_ffn_weight_fp8, dtype=np.float32))
    bias = np.ascontiguousarray(np.asarray(L0_ffn_bias, dtype=np.float32)
                                .reshape(1, 1024))
    si = np.ascontiguousarray(np.asarray(L0_ffn_input_scale, dtype=np.float32)
                              .reshape(1, 1))
    sw = np.ascontiguousarray(np.asarray(L0_ffn_weight_scale, dtype=np.float32)
                              .reshape(1, 1))
    wt = np.ascontiguousarray(w.T)

    if "nc" not in _NC_CACHE:
        _NC_CACHE["nc"] = build_program()
    nc = _NC_CACHE["nc"]

    in_maps = []
    for c in range(8):
        xc = x[4 * c:4 * c + 4]
        in_maps.append({
            "x": np.ascontiguousarray(xc.reshape(SEQ * 2048, 1024)),
            "xt": np.ascontiguousarray(xc.transpose(0, 2, 1)).reshape(
                SEQ * 1024, 2048),
            "qp": qp, "kp": kp, "wt": wt, "bias": bias, "si": si, "sw": sw,
        })
    res = run_bass_kernel_spmd(nc, in_maps, core_ids=list(range(8)))
    out = np.concatenate([r["out"].reshape(4, 2048, 1024)
                          for r in res.results], axis=0)
    return out
